# version 04
# baseline (speedup 1.0000x reference)
import contextlib
import numpy as np
import ml_dtypes
import concourse.bass as bass
import concourse.mybir as mybir
from concourse.bass_utils import run_bass_kernel_spmd

F32 = mybir.dt.float32
BF16 = mybir.dt.bfloat16
AX = mybir.AxisListType.X
ALU = mybir.AluOpType
ACT = mybir.ActivationFunctionType

D = 1024
T = 2048
H = 16
NE = 16
NEX = 17
EXP_SHIFT = 18.0
LN_EPS = 1e-5
ROUTE_SCALE = 2.5

N_W13P = 8
N_W2P = 4
N_PIECE = N_W13P + N_W2P

PIECES = [(0, j) for j in range(N_PIECE)] + [(1, j) for j in range(N_W13P)]
for _e in range(1, NEX):
    PIECES += [(_e, N_W13P + j) for j in range(N_W2P)]
    if _e + 1 < NEX:
        PIECES += [(_e + 1, j) for j in range(N_W13P)]
assert len(PIECES) == NEX * N_PIECE
N_PROLOG = N_PIECE + N_W13P


THR13 = {e: max(i for i, (ee, j) in enumerate(PIECES) if ee == e and j < N_W13P) + 1
         for e in range(NEX)}
THR2 = {e: max(i for i, (ee, j) in enumerate(PIECES) if ee == e and j >= N_W13P) + 1
        for e in range(NEX)}


def piece_slice(s):
    rem = PIECES[N_PROLOG:]
    i0 = (s - 4) * 3
    return list(enumerate(rem[i0:i0 + 3], start=N_PROLOG + i0)) if i0 < len(rem) else []


def build_nc(ph=4, p3n=3, p3spill=True):
    nc = bass.Bass("TRN2", target_bir_lowering=False, debug=False)

    xT = nc.dram_tensor("xT", [D, T], F32, kind="ExternalInput").ap()
    wqkv = nc.dram_tensor("wqkv", [D, 3 * D], F32, kind="ExternalInput").ap()
    wout_d = nc.dram_tensor("wout", [D, D], F32, kind="ExternalInput").ap()
    bout_d = nc.dram_tensor("bout", [128, 8], F32, kind="ExternalInput").ap()
    gwT_d = nc.dram_tensor("gwT", [D, 16], F32, kind="ExternalInput").ap()
    gb_d = nc.dram_tensor("gb", [1, 256], F32, kind="ExternalInput").ap()
    w13T_d = nc.dram_tensor("w13T", [NE, D, D], F32, kind="ExternalInput").ap()
    w2T_d = nc.dram_tensor("w2T", [NE, 512, D], F32, kind="ExternalInput").ap()
    w13sT_d = nc.dram_tensor("w13sT", [D, D], F32, kind="ExternalInput").ap()
    w2sT_d = nc.dram_tensor("w2sT", [512, D], F32, kind="ExternalInput").ap()
    revio_d = nc.dram_tensor("revio", [128, 256], F32, kind="ExternalInput").ap()
    ones1f_d = nc.dram_tensor("ones1f", [1, 128], F32, kind="ExternalInput").ap()
    ones1b_d = nc.dram_tensor("ones1b", [1, 128], BF16, kind="ExternalInput").ap()
    onescol_d = nc.dram_tensor("onescol", [128, 1], BF16, kind="ExternalInput").ap()
    ident_d = nc.dram_tensor("ident", [128, 128], F32, kind="ExternalInput").ap()
    ehot_d = nc.dram_tensor("ehot", [16, 16 * 128], BF16, kind="ExternalInput").ap()
    outT = nc.dram_tensor("outT", [D, T], F32, kind="ExternalOutput").ap()
    qscr = nc.dram_tensor("qscr", [128, 8 * T], BF16, kind="Internal").ap()
    kscr = nc.dram_tensor("kscr", [128, 8 * T], BF16, kind="Internal").ap()

    st = contextlib.ExitStack()
    E = st.enter_context
    with st:
        y_acc = E(nc.sbuf_tensor("y_acc", [128, 8 * T], F32))
        vhat = E(nc.sbuf_tensor("vhat", [128, 16 * 16 * 65], BF16))
        gwT_sb = E(nc.sbuf_tensor("gwT_sb", [128, 8 * 16], F32))
        gwTb = E(nc.sbuf_tensor("gwTb", [128, 8 * 16], BF16))
        bout_sb = E(nc.sbuf_tensor("bout_sb", [128, 8], F32))
        gb_sb = E(nc.sbuf_tensor("gb_sb", [1, 256], F32))
        gbb_sb = E(nc.sbuf_tensor("gbb_sb", [128, 256], F32))
        revio_sb = E(nc.sbuf_tensor("revio_sb", [128, 256], F32))
        ones1f_sb = E(nc.sbuf_tensor("ones1f_sb", [1, 128], F32))
        ones1b_sb = E(nc.sbuf_tensor("ones1b_sb", [1, 128], BF16))
        onescol_sb = E(nc.sbuf_tensor("onescol_sb", [128, 1], BF16))
        ident_sb = E(nc.sbuf_tensor("ident_sb", [128, 128], F32))
        eps_t = E(nc.sbuf_tensor("eps_t", [1, 1], F32))
        bexp = E(nc.sbuf_tensor("bexp", [128, 1], F32))
        ehot_sb = E(nc.sbuf_tensor("ehot_sb", [16, 16 * 128], BF16))

        pxn = contextlib.ExitStack()
        combTb = pxn.enter_context(nc.sbuf_tensor("combTb", [16, T], BF16))
        xn = pxn.enter_context(nc.sbuf_tensor("xn", [128, 8 * T], BF16))

        p1 = contextlib.ExitStack()
        E1 = p1.enter_context
        combT = E1(nc.sbuf_tensor("combT", [16, T], F32))
        rbt = E1(nc.sbuf_tensor("rbt", [128, T], F32))
        mrb = E1(nc.sbuf_tensor("mrb", [128, T], F32))
        sxrow = E1(nc.sbuf_tensor("sxrow", [1, T], F32))
        sx2row = E1(nc.sbuf_tensor("sx2row", [1, T], F32))
        tmp_r = [E1(nc.sbuf_tensor(f"tmp_r{i}", [128, 512], F32)) for i in range(2)]
        xb_r = [E1(nc.sbuf_tensor(f"xb_r{i}", [128, 512], BF16)) for i in range(2)]
        sq_r = [E1(nc.sbuf_tensor(f"sq_r{i}", [128, 512], BF16)) for i in range(2)]
        scores_all = E1(nc.sbuf_tensor("scores_all", [128, 256], F32))
        comb_tok = E1(nc.sbuf_tensor("comb_tok", [128, 256], F32))
        work_all = E1(nc.sbuf_tensor("work_all", [128, 256], F32))
        eqm_all = E1(nc.sbuf_tensor("eqm_all", [128, 256], F32))
        sel_all = E1(nc.sbuf_tensor("sel_all", [128, 256], F32))
        col1_all = E1(nc.sbuf_tensor("col1_all", [128, 16], F32))
        col2_all = E1(nc.sbuf_tensor("col2_all", [128, 16], F32))
        p1_psum = contextlib.ExitStack()
        EP1 = p1_psum.enter_context
        stat_x = EP1(nc.psum_tensor("stat_x", [1, 512], F32))
        stat_x2 = EP1(nc.psum_tensor("stat_x2", [1, 512], F32))
        rps = EP1(nc.psum_tensor("rps", [128, 256], F32))
        bc_ps = [EP1(nc.psum_tensor(f"bc_ps{i}", [128, 512], F32)) for i in range(2)]
        tps = [EP1(nc.psum_tensor(f"tps{i}", [16, 128], F32)) for i in range(2)]
        gb_ps = EP1(nc.psum_tensor("gb_ps", [128, 256], F32))
        p1_psum.close()
        p1.close()

        p2 = contextlib.ExitStack()
        E2 = p2.enter_context
        stage = [E2(nc.sbuf_tensor(f"stage{i}", [128, 1024], F32)) for i in range(2)]
        w13b = [E2(nc.sbuf_tensor(f"w13b{i}", [128, 8 * 1024], BF16)) for i in range(2)]
        w2b = E2(nc.sbuf_tensor("w2b", [128, 4 * 1024], BF16))
        hpsb = [E2(nc.sbuf_tensor(f"hpsb{i}", [128, 4 * 512], BF16)) for i in range(2)]
        silu_sb = [E2(nc.sbuf_tensor(f"silu_sb{i}", [128, 512], F32)) for i in range(2)]
        t1_r = [E2(nc.sbuf_tensor(f"t1_r{i}", [128, 512], F32)) for i in range(2)]
        p2_psum = contextlib.ExitStack()
        EP2 = p2_psum.enter_context
        cbp = [EP2(nc.psum_tensor(f"cbp{i}", [128, 512], F32)) for i in range(2)]
        hp_g = [EP2(nc.psum_tensor(f"hp_g{i}", [128, 512], F32)) for i in range(2)]
        hp_u = [EP2(nc.psum_tensor(f"hp_u{i}", [128, 512], F32)) for i in range(2)]
        ye = [EP2(nc.psum_tensor(f"ye{i}", [128, 512], F32)) for i in range(2)]
        p2_psum.close()
        p2.close()

        p3 = contextlib.ExitStack()
        E3 = p3.enter_context
        wpart = [E3(nc.sbuf_tensor(f"wpart{i}", [128, 8 * 1024], BF16)) for i in range(2)]
        p3st = [E3(nc.sbuf_tensor(f"p3st{i}", [128, 1024], F32)) for i in range(2)]
        qblk = [E3(nc.sbuf_tensor(f"qblk{i}", [128, T], BF16)) for i in range(2)]
        p3_psum = contextlib.ExitStack()
        EP3 = p3_psum.enter_context
        qk_ps = [EP3(nc.psum_tensor(f"qk_ps{i}", [128, 512], F32)) for i in range(3)]
        v_ps = [EP3(nc.psum_tensor(f"v_ps{i}", [128, 512], F32)) for i in range(2)]
        p3_psum.close()
        p3.close()
        pxn.close()

        p4 = contextlib.ExitStack()
        E4 = p4.enter_context
        k_sb = E4(nc.sbuf_tensor("k_sb", [128, 8 * T], BF16))
        wout_b = E4(nc.sbuf_tensor("wout_b", [128, 8 * 1024], BF16))
        wost = [E4(nc.sbuf_tensor(f"wost{i}", [128, 1024], F32)) for i in range(2)]
        qb4 = [E4(nc.sbuf_tensor(f"qb4_{i}", [128, 8 * 512], BF16)) for i in range(2)]
        PT_E = E4(nc.sbuf_tensor("PT_E", [128, 2 * 1024], BF16))
        PT_O = E4(nc.sbuf_tensor("PT_O", [128, 2 * 1024], BF16))
        attnN = E4(nc.sbuf_tensor("attnN", [128, 8 * 512], BF16))
        bcE_sb = E4(nc.sbuf_tensor("bcE_sb", [64, 512], F32))
        bcO_sb = E4(nc.sbuf_tensor("bcO_sb", [64, 512], F32))
        rdenE = E4(nc.sbuf_tensor("rdenE", [1, 512], BF16))
        rdenO = E4(nc.sbuf_tensor("rdenO", [1, 512], BF16))
        p4_psum = contextlib.ExitStack()
        EP4 = p4_psum.enter_context
        scE = EP4(nc.psum_tensor("scE", [128, 1024], F32))
        scO = EP4(nc.psum_tensor("scO", [128, 1024], F32))
        pvE = EP4(nc.psum_tensor("pvE", [128, 512], F32))
        pvO = EP4(nc.psum_tensor("pvO", [128, 512], F32))
        bcyo = [EP4(nc.psum_tensor(f"bcyo{i}", [128, 512], F32)) for i in range(2)]
        p4_psum.close()
        p4.close()

        names = ("sX sC sMs sXb sSq sStM sStE sRow1 sRow2 sRow3 sBcM sBcE sXn "
                 "sBout sGw sGbM sGbE sRt sSig sCombTok sCT sCTE sCTb "
                 "sStgL0 sStgL1 sStgF sW13L sW13F sW2L sW2F sCbM sCbF sCbR sHpL sHpF sSl "
                 "sHpD sMM2D sYeL sYeF sFFNd "
                 "sQSL0 sQSL1 sQSF sWPL sWPF sQkM sQkE sVM sVE sBlk sSpl0 sSpl1 sP3V "
                 "sKL sWoL0 sWoL1 sWoC sQL0 sQL1 sQF "
                 "sScE sScO sExpE sExpO sPTFE sPTFO sPvLE sPvLO sRdE sRdO "
                 "sBcAE sBcAO sPvFE sPvFO sAN sYoL sYoF sOut").split()
        S = {n: E(nc.semaphore(n)) for n in names}

        block = E(nc.Block())

        @block.sync
        def _(sync):
            sync.dma_start(
                out=y_acc.ap().rearrange("p (c t) -> p c t", c=8),
                in_=xT.rearrange("(c p) t -> p c t", p=128),
            ).then_inc(S["sX"], 16)
            sync.dma_start(
                out=gwT_sb.ap().rearrange("p (c e) -> p c e", c=8),
                in_=gwT_d.rearrange("(c p) e -> p c e", p=128),
            ).then_inc(S["sC"], 16)
            sync.dma_start(out=bout_sb[:], in_=bout_d[:, :]).then_inc(S["sC"], 16)
            sync.dma_start(out=gb_sb[:], in_=gb_d[:, :]).then_inc(S["sC"], 16)
            sync.dma_start(out=revio_sb[:], in_=revio_d[:, :]).then_inc(S["sC"], 16)
            sync.dma_start(out=ones1f_sb[:], in_=ones1f_d[:, :]).then_inc(S["sC"], 16)
            sync.dma_start(out=ones1b_sb[:], in_=ones1b_d[:, :]).then_inc(S["sC"], 16)
            sync.dma_start(out=onescol_sb[:], in_=onescol_d[:, :]).then_inc(S["sC"], 16)
            sync.dma_start(out=ident_sb[:], in_=ident_d[:, :]).then_inc(S["sC"], 16)
            sync.dma_start(out=ehot_sb[:], in_=ehot_d[:, :]).then_inc(S["sC"], 16)

            if ph < 2:
                if ph >= 1:
                    sync.wait_ge(S["sBout"], 1)
                    sync.wait_ge(S["sCTb"], 1)
                sync.dma_start(
                    out=outT.rearrange("(c p) t -> p c t", p=128),
                    in_=y_acc.ap().rearrange("p (c t) -> p c t", c=8),
                ).then_inc(S["sOut"], 16)
                sync.wait_ge(S["sOut"], 16)
                return
            sync.wait_ge(S["sCTb"], 1)
            for g, (e, j) in enumerate(PIECES):
                if g >= 2:
                    sync.wait_ge(S["sStgF"], g - 1)
                if j < N_W13P:
                    src = (w13T_d[e, 128 * j:128 * (j + 1), :] if e < NE
                           else w13sT_d[128 * j:128 * (j + 1), :])
                else:
                    jj = j - N_W13P
                    src = (w2T_d[e, 128 * jj:128 * (jj + 1), :] if e < NE
                           else w2sT_d[128 * jj:128 * (jj + 1), :])
                sync.dma_start(out=stage[g % 2][:], in_=src).then_inc(S[f"sStgL{g % 2}"], 16)

            if ph < 3:
                sync.wait_ge(S["sYeF"], 544)
                sync.dma_start(
                    out=outT.rearrange("(c p) t -> p c t", p=128),
                    in_=y_acc.ap().rearrange("p (c t) -> p c t", c=8),
                ).then_inc(S["sOut"], 16)
                sync.wait_ge(S["sOut"], 16)
                return
            sync.wait_ge(S["sStgF"], len(PIECES))
            sync.wait_ge(S["sYeL"], 544)
            def p3_loads(p):
                for c in range(8):
                    g = p * 8 + c
                    if g >= 2:
                        sync.wait_ge(S["sQSF"], g - 1)
                    sync.dma_start(
                        out=p3st[g % 2][:],
                        in_=wqkv[128 * c:128 * (c + 1), 1024 * p:1024 * (p + 1)],
                    ).then_inc(S[f"sQSL{g % 2}"], 16)

            def p3_spills(part):
                if not p3spill:
                    return
                for m in range(8 * part, 8 * (part + 1)):
                    sync.wait_ge(S["sBlk"], m + 1)
                    dst = qscr if m < 8 else kscr
                    oc = m % 8
                    sync.dma_start(out=dst[:, T * oc:T * (oc + 1)],
                                   in_=qblk[m % 2][:]).then_inc(S[f"sSpl{m % 2}"], 16)

            p3_loads(0)
            if p3n >= 2:
                p3_loads(1)
            p3_spills(0)
            if p3n >= 3:
                p3_loads(2)
            if p3n >= 2:
                p3_spills(1)

            if ph < 4:
                sync.wait_ge(S["sP3V"], 1)
                if p3spill:
                    nsp = 8 * min(p3n, 2)
                    sync.wait_ge(S["sSpl0"], 16 * ((nsp + 1) // 2))
                    sync.wait_ge(S["sSpl1"], 16 * (nsp // 2))
                sync.dma_start(
                    out=outT.rearrange("(c p) t -> p c t", p=128),
                    in_=y_acc.ap().rearrange("p (c t) -> p c t", c=8),
                ).then_inc(S["sOut"], 16)
                sync.wait_ge(S["sOut"], 16)
                return
            sync.wait_ge(S["sP3V"], 1)
            sync.wait_ge(S["sSpl0"], 128)
            sync.wait_ge(S["sSpl1"], 128)
            sync.dma_start(out=k_sb[:], in_=kscr[:, :]).then_inc(S["sKL"], 16)
            for j in range(8):
                if j >= 2:
                    sync.wait_ge(S["sWoC"], j - 1)
                sync.dma_start(out=wost[j % 2][:],
                               in_=wout_d[128 * j:128 * (j + 1), :]).then_inc(S[f"sWoL{j % 2}"], 16)
            for t1b in range(4):
                if t1b >= 2:
                    sync.wait_ge(S["sScO"], 64 * (t1b - 1))
                sync.dma_start(
                    out=qb4[t1b % 2].ap().rearrange("p (c t) -> p c t", c=8),
                    in_=qscr.rearrange("p (c t) -> p c t", c=8)[:, :, 512 * t1b:512 * (t1b + 1)],
                ).then_inc(S[f"sQL{t1b % 2}"], 16)

            sync.wait_ge(S["sYoF"], 32)
            sync.dma_start(
                out=outT.rearrange("(c p) t -> p c t", p=128),
                in_=y_acc.ap().rearrange("p (c t) -> p c t", c=8),
            ).then_inc(S["sOut"], 16)
            sync.wait_ge(S["sOut"], 16)

        @block.tensor
        def _(tensor):
            tensor.wait_ge(S["sC"], 144)
            tensor.matmul(gb_ps[:], ones1f_sb[:], gb_sb[:],
                          start=True, stop=True).then_inc(S["sGbM"], 1)
            for nb in range(4):
                if nb > 0:
                    tensor.wait_ge(S["sStE"], nb)
                for c in range(8):
                    idx = nb * 8 + c
                    tensor.wait_ge(S["sXb"], idx + 1)
                    tensor.matmul(stat_x[:], onescol_sb[:], xb_r[idx % 2][:],
                                  start=(c == 0), stop=(c == 7))
                    tensor.wait_ge(S["sSq"], idx + 1)
                    tensor.matmul(stat_x2[:], onescol_sb[:], sq_r[idx % 2][:],
                                  start=(c == 0), stop=(c == 7)).then_inc(S["sStM"], 1)
            tensor.wait_ge(S["sRow2"], 1)
            tensor.wait_ge(S["sRow3"], 1)
            for nb in range(4):
                for half, row in enumerate((sx2row, sxrow)):
                    i = nb * 2 + half
                    if i >= 2:
                        tensor.wait_ge(S["sBcE"], i - 1)
                    tensor.matmul(bc_ps[i % 2][:], ones1f_sb[:],
                                  row[0:1, 512 * nb:512 * (nb + 1)],
                                  start=True, stop=True).then_inc(S["sBcM"], 1)
            tensor.wait_ge(S["sXn"], 8)
            tensor.wait_ge(S["sGw"], 1)
            for tc in range(16):
                for c in range(8):
                    mm = tensor.matmul(rps[:, 16 * tc:16 * (tc + 1)],
                                       xn[:, T * c + 128 * tc:T * c + 128 * (tc + 1)],
                                       gwTb[:, 16 * c:16 * (c + 1)],
                                       start=(c == 0), stop=(c == 7))
                    if tc == 15 and c == 7:
                        mm.then_inc(S["sRt"], 1)
            for tc in range(16):
                if tc == 0:
                    tensor.wait_ge(S["sCombTok"], 1)
                if tc >= 2:
                    tensor.wait_ge(S["sCTE"], tc - 1)
                tensor.transpose(tps[tc % 2][:], comb_tok[:, 16 * tc:16 * (tc + 1)],
                                 ident_sb[:]).then_inc(S["sCT"], 1)

            if ph < 2:
                return
            tensor.wait_ge(S["sCTb"], 1)

            def mm2_of(sp):
                ep, tbp = divmod(sp, 4)
                if tbp == 0:
                    tensor.wait_ge(S["sStgF"], THR2[ep])
                tensor.wait_ge(S["sHpD"], 4 * (sp + 1))
                for dc in range(8):
                    gd = sp * 8 + dc
                    if gd >= 2:
                        tensor.wait_ge(S["sYeF"], gd - 1)
                    for jc in range(4):
                        mm = tensor.matmul(
                            ye[gd % 2][:],
                            w2b[:, 1024 * jc + 128 * dc:1024 * jc + 128 * (dc + 1)],
                            hpsb[sp % 2][:, 512 * jc:512 * (jc + 1)],
                            start=(jc == 0), stop=(jc == 3))
                        if jc == 3:
                            mm.then_inc(S["sYeL"], 1)


            for s in range(4 * NEX):
                e, tb = divmod(s, 4)
                if e < NE:
                    if s >= 2:
                        tensor.wait_ge(S["sHpD"], 4 * s - 4)
                    tensor.wait_ge(S["sCTb"], 1)
                    tensor.matmul(cbp[s % 2][:], ehot_sb[:, 128 * e:128 * (e + 1)],
                                  combTb[0:16, 512 * tb:512 * (tb + 1)],
                                  start=True, stop=True).then_inc(S["sCbM"], 1)
                if tb == 0:
                    tensor.wait_ge(S["sStgF"], THR13[e])
                for pr in range(4):
                    P = s * 4 + pr
                    if P >= 2:
                        tensor.wait_ge(S["sHpF"], P - 1)
                    for c in range(8):
                        mm = tensor.matmul(
                            hp_g[P % 2][:],
                            w13b[e % 2][:, 1024 * c + 128 * pr:1024 * c + 128 * (pr + 1)],
                            xn[:, T * c + 512 * tb:T * c + 512 * (tb + 1)],
                            start=(c == 0), stop=(c == 7))
                        if c == 7:
                            mm.then_inc(S["sHpL"], 1)
                    for c in range(8):
                        mm = tensor.matmul(
                            hp_u[P % 2][:],
                            w13b[e % 2][:, 1024 * c + 128 * (pr + 4):1024 * c + 128 * (pr + 5)],
                            xn[:, T * c + 512 * tb:T * c + 512 * (tb + 1)],
                            start=(c == 0), stop=(c == 7))
                        if c == 7:
                            mm.then_inc(S["sHpL"], 1)
                if s >= 1:
                    mm2_of(s - 1)
            mm2_of(4 * NEX - 1)

            if ph < 3:
                return
            for p in range(p3n):
                tensor.wait_ge(S["sQSF"], 8 * (p + 1))
                if p < 2:
                    for oc in range(8):
                        for nb in range(4):
                            gqk = (p * 8 + oc) * 4 + nb
                            if gqk >= 3:
                                tensor.wait_ge(S["sQkE"], gqk - 2)
                            for c in range(8):
                                mm = tensor.matmul(
                                    qk_ps[gqk % 3][:],
                                    wpart[p % 2][:, 1024 * c + 128 * oc:1024 * c + 128 * (oc + 1)],
                                    xn[:, T * c + 512 * nb:T * c + 512 * (nb + 1)],
                                    start=(c == 0), stop=(c == 7))
                                if c == 7:
                                    mm.then_inc(S["sQkM"], 1)
                else:
                    for tc in range(16):
                        for nh in range(2):
                            gv = tc * 2 + nh
                            if gv >= 2:
                                tensor.wait_ge(S["sVE"], gv - 1)
                            for c in range(8):
                                mm = tensor.matmul(
                                    v_ps[gv % 2][:],
                                    xn[:, T * c + 128 * tc:T * c + 128 * (tc + 1)],
                                    wpart[0][:, 1024 * c + 512 * nh:1024 * c + 512 * (nh + 1)],
                                    start=(c == 0), stop=(c == 7))
                                if c == 7:
                                    mm.then_inc(S["sVM"], 1)

            if ph < 4:
                return
            tensor.wait_ge(S["sKL"], 16)
            for t1b in range(4):
                tensor.wait_ge(S[f"sQL{t1b % 2}"], 16 * (t1b // 2 + 1))
                for hp in range(8):
                    n = t1b * 8 + hp

                    def pv_pair(gp):
                        for par, (pv, PT, sPTF, sExp, sPvF) in enumerate((
                                (pvE, PT_E, "sPTFE", "sExpE", "sPvFE"),
                                (pvO, PT_O, "sPTFO", "sExpO", "sPvFO"))):
                            h = 2 * hp + par
                            tensor.wait_ge(S[sExp], n * 8 + gp + 1)
                            if gp == 0:
                                tensor.wait_ge(S[sPvF], n)
                            slot = (n * 8 + gp) % 2
                            for k2 in range(2):
                                t2c = 2 * gp + k2
                                mm = tensor.matmul(
                                    pv[0:65, :],
                                    vhat[:, 1040 * t2c + 65 * h:1040 * t2c + 65 * (h + 1)],
                                    PT[:, 1024 * slot + 512 * k2:1024 * slot + 512 * (k2 + 1)],
                                    start=(gp == 0 and k2 == 0),
                                    stop=(gp == 7 and k2 == 1))
                                if k2 == 1:
                                    mm.then_inc(S[sPTF], 1)

                    for g in range(8):
                        for par, (sc, sScS, sExp) in enumerate(
                                ((scE, "sScE", "sExpE"), (scO, "sScO", "sExpO"))):
                            h = 2 * hp + par
                            po = 64 * par
                            tensor.wait_ge(S[sExp], n * 8 + g)
                            tpos = (64, 0) if par == 1 else None
                            for k2 in range(2):
                                t2c = 2 * g + k2
                                mm = tensor.matmul(
                                    sc[:, 512 * k2:512 * (k2 + 1)],
                                    k_sb[po:po + 64, T * hp + 128 * t2c:T * hp + 128 * (t2c + 1)],
                                    qb4[t1b % 2][po:po + 64, 512 * hp:512 * (hp + 1)],
                                    start=True, stop=True, tile_position=tpos)
                                if k2 == 1:
                                    mm.then_inc(S[sScS], 1)
                        if g >= 1:
                            pv_pair(g - 1)
                    pv_pair(7)
                    tensor.wait_ge(S["sRdE"], n + 1)
                    if hp == 0:
                        tensor.wait_ge(S["sYoF"], 8 * t1b)
                    tensor.matmul(bcyo[0][0:64, :], ones1b_sb[0:1, 0:64], rdenE[:],
                                  start=True, stop=True).then_inc(S["sBcAE"], 1)
                    tensor.wait_ge(S["sRdO"], n + 1)
                    tensor.matmul(bcyo[1][0:64, :], ones1b_sb[0:1, 0:64], rdenO[:],
                                  start=True, stop=True).then_inc(S["sBcAO"], 1)
                tensor.wait_ge(S["sPvFE"], 8 * (t1b + 1))
                tensor.wait_ge(S["sPvFO"], 8 * (t1b + 1))
                if t1b == 0:
                    tensor.wait_ge(S["sWoC"], 8)
                for dc in range(8):
                    gd = t1b * 8 + dc
                    if gd >= 2:
                        tensor.wait_ge(S["sYoF"], gd - 1)
                    for ic in range(8):
                        mm = tensor.matmul(
                            bcyo[gd % 2][:],
                            wout_b[:, 1024 * ic + 128 * dc:1024 * ic + 128 * (dc + 1)],
                            attnN[:, 512 * ic:512 * (ic + 1)],
                            start=(ic == 0), stop=(ic == 7))
                        if ic == 7:
                            mm.then_inc(S["sYoL"], 1)

        @block.scalar
        def _(scalar):
            scalar.wait_ge(S["sX"], 16)
            scalar.wait_ge(S["sC"], 144)
            scalar.copy(out=gwTb[:], in_=gwT_sb[:]).then_inc(S["sGw"], 1)
            for nb in range(4):
                for c in range(8):
                    idx = nb * 8 + c
                    if idx >= 2:
                        scalar.wait_ge(S["sStM"], idx - 1)
                    scalar.copy(out=xb_r[idx % 2][:],
                                in_=y_acc.ap()[:, T * c + 512 * nb:T * c + 512 * (nb + 1)]
                                ).then_inc(S["sXb"], 1)
            scalar.wait_ge(S["sRow1"], 1)
            scalar.activation(sx2row[:], sx2row[:], ACT.Ln, bias=eps_t[0:1, 0:1])
            scalar.drain()
            scalar.activation(sx2row[:], sx2row[:], ACT.Exp, scale=-0.5).then_inc(S["sRow2"], 1)
            scalar.wait_ge(S["sRt"], 1)
            scalar.activation(scores_all[:], rps[:], ACT.Sigmoid).then_inc(S["sSig"], 1)
            scalar.wait_ge(S["sCTE"], 16)
            scalar.copy(out=combTb[:], in_=combT[:]).then_inc(S["sCTb"], 1)

            if ph < 2:
                return
            def cast_piece(g, e, j):
                scalar.wait_ge(S[f"sStgL{g % 2}"], 16 * (g // 2 + 1))
                if g == 0:
                    scalar.wait_ge(S["sXn"], 8)
                    scalar.wait_ge(S["sBcM"], 8)
                if j < N_W13P:
                    if j == 0 and e >= 2:
                        scalar.wait_ge(S["sYeL"], 32 * (e - 2) + 24)
                    scalar.copy(out=w13b[e % 2][:, 1024 * j:1024 * (j + 1)],
                                in_=stage[g % 2][:]).then_inc(S["sStgF"], 1)
                else:
                    if j == N_W13P and e >= 1:
                        scalar.wait_ge(S["sYeL"], 32 * e)
                    scalar.copy(out=w2b[:, 1024 * (j - N_W13P):1024 * (j - N_W13P + 1)],
                                in_=stage[g % 2][:]).then_inc(S["sStgF"], 1)

            for g in range(N_PROLOG):
                cast_piece(g, *PIECES[g])
            for s in range(4 * NEX):
                for pr in range(4):
                    P = s * 4 + pr
                    scalar.wait_ge(S["sHpL"], 2 * P + 1)
                    scalar.activation(silu_sb[P % 2][:], hp_g[P % 2][:],
                                      ACT.Silu).then_inc(S["sSl"], 1)
                if s >= 4:
                    for g, (e, j) in piece_slice(s):
                        cast_piece(g, e, j)

            if ph < 3:
                return
            for p in range(p3n):
                for c in range(8):
                    g = p * 8 + c
                    scalar.wait_ge(S[f"sQSL{g % 2}"], 16 * (g // 2 + 1))
                    if g == 0:
                        scalar.wait_ge(S["sYeL"], 544)
                    if p == 2 and c == 0:
                        scalar.wait_ge(S["sQkM"], 32)
                    scalar.copy(out=wpart[p % 2][:, 1024 * c:1024 * (c + 1)],
                                in_=p3st[g % 2][:]).then_inc(S["sQSF"], 1)

            if ph < 4:
                return
            scalar.wait_ge(S["sYeL"], 544)
            for j in range(8):
                scalar.wait_ge(S[f"sWoL{j % 2}"], 16 * (j // 2 + 1))
                scalar.copy(out=wout_b[:, 1024 * j:1024 * (j + 1)],
                            in_=wost[j % 2][:]).then_inc(S["sWoC"], 1)
            scalar.wait_ge(S["sMs"], 1)
            for t1b in range(4):
                for hp in range(8):
                    n = t1b * 8 + hp
                    for g in range(8):
                        gE = n * 8 + g
                        for (sc, PT, sScS, sExp, sPTF) in (
                                (scE, PT_E, "sScE", "sExpE", "sPTFE"),
                                (scO, PT_O, "sScO", "sExpO", "sPTFO")):
                            scalar.wait_ge(S[sScS], gE + 1)
                            if gE >= 2:
                                scalar.wait_ge(S[sPTF], gE - 1)
                            slot = gE % 2
                            scalar.activation(PT[:, 1024 * slot:1024 * (slot + 1)],
                                              sc[:], ACT.Exp, bias=bexp[:, 0:1],
                                              scale=0.125).then_inc(S[sExp], 1)

        @block.vector
        def _(vector):
            vector.memset(eps_t[:], LN_EPS)
            vector.memset(bexp[:], -EXP_SHIFT).then_inc(S["sMs"], 1)
            vector.wait_ge(S["sX"], 16)
            vector.wait_ge(S["sC"], 144)
            vector.wait_ge(S["sGbM"], 1)
            vector.tensor_copy(out=gbb_sb[:], in_=gb_ps[:]).then_inc(S["sGbE"], 1)
            for nb in range(4):
                for c in range(8):
                    idx = nb * 8 + c
                    vector.wait_ge(S["sXb"], idx + 1)
                    if idx >= 2:
                        vector.wait_ge(S["sStM"], idx - 1)
                    vector.tensor_mul(sq_r[idx % 2][:], xb_r[idx % 2][:],
                                      xb_r[idx % 2][:]).then_inc(S["sSq"], 1)
                vector.wait_ge(S["sStM"], 8 * (nb + 1))
                vector.tensor_copy(out=sxrow[0:1, 512 * nb:512 * (nb + 1)], in_=stat_x[:])
                vector.tensor_copy(out=sx2row[0:1, 512 * nb:512 * (nb + 1)],
                                   in_=stat_x2[:]).then_inc(S["sStE"], 1)
            vector.drain()
            vector.tensor_scalar_mul(sxrow[:], sxrow[:], 1.0 / D)
            vector.tensor_scalar_mul(sx2row[:], sx2row[:], 1.0 / D)
            vector.drain()
            vector.tensor_mul(rbt.ap()[0:1, 0:T], sxrow[:], sxrow[:])
            vector.drain()
            vector.tensor_sub(sx2row[:], sx2row[:], rbt.ap()[0:1, 0:T]).then_inc(S["sRow1"], 1)
            vector.wait_ge(S["sRow2"], 1)
            vector.tensor_mul(sxrow[:], sxrow[:], sx2row[:]).then_inc(S["sRow3"], 1)
            for i in range(8):
                vector.wait_ge(S["sBcM"], i + 1)
                dst = rbt if i % 2 == 0 else mrb
                vector.tensor_copy(out=dst.ap()[:, 512 * (i // 2):512 * (i // 2 + 1)],
                                   in_=bc_ps[i % 2][:]).then_inc(S["sBcE"], 1)
            vector.drain()
            for c in range(8):
                for nb in range(4):
                    q = c * 4 + nb
                    sl = slice(T * c + 512 * nb, T * c + 512 * (nb + 1))
                    bsl = slice(512 * nb, 512 * (nb + 1))
                    vector.tensor_mul(tmp_r[q % 2][:], y_acc.ap()[:, sl], rbt.ap()[:, bsl])
                    vector.drain()
                    ins = vector.tensor_sub(xn.ap()[:, sl], tmp_r[q % 2][:], mrb.ap()[:, bsl])
                    if nb == 3:
                        ins.then_inc(S["sXn"], 1)
            vector.drain()
            for c in range(8):
                ins = vector.tensor_scalar_add(y_acc.ap()[:, T * c:T * (c + 1)],
                                               y_acc.ap()[:, T * c:T * (c + 1)],
                                               bout_sb[:, c:c + 1])
                if c == 7:
                    ins.then_inc(S["sBout"], 1)
            vector.wait_ge(S["sSig"], 1)
            vector.wait_ge(S["sGbE"], 1)
            vector.drain()
            vector.tensor_add(work_all[:], scores_all[:], gbb_sb[:])
            vector.memset(comb_tok[:], 0.0)
            w3d = work_all.ap().rearrange("p (a e) -> p a e", e=16)
            e3d = eqm_all.ap().rearrange("p (a e) -> p a e", e=16)
            c3d = comb_tok.ap().rearrange("p (a e) -> p a e", e=16)
            for k in range(4):
                vector.drain()
                vector.reduce_max(col1_all[:], w3d, axis=AX)
                vector.drain()
                for tc in range(16):
                    sl = slice(16 * tc, 16 * (tc + 1))
                    vector.tensor_single_scalar(eqm_all.ap()[:, sl], work_all.ap()[:, sl],
                                                col1_all[:, tc:tc + 1], ALU.is_ge)
                vector.drain()
                vector.tensor_mul(eqm_all[:], eqm_all[:], revio_sb[:])
                vector.drain()
                vector.reduce_max(col2_all[:], e3d, axis=AX)
                vector.drain()
                for tc in range(16):
                    sl = slice(16 * tc, 16 * (tc + 1))
                    vector.tensor_single_scalar(sel_all.ap()[:, sl], eqm_all.ap()[:, sl],
                                                col2_all[:, tc:tc + 1], ALU.is_ge)
                vector.drain()
                vector.scalar_tensor_tensor(work_all[:], sel_all[:], -1e30, work_all[:],
                                            ALU.mult, ALU.add)
                vector.drain()
                vector.tensor_mul(sel_all[:], sel_all[:], scores_all[:])
                vector.drain()
                vector.tensor_add(comb_tok[:], comb_tok[:], sel_all[:])
            vector.drain()
            vector.reduce_sum(col1_all[:], c3d, axis=AX)
            vector.drain()
            vector.reciprocal(col2_all[:], col1_all[:])
            vector.drain()
            for tc in range(16):
                sl = slice(16 * tc, 16 * (tc + 1))
                ins = vector.tensor_scalar(comb_tok.ap()[:, sl], comb_tok.ap()[:, sl],
                                           col2_all[:, tc:tc + 1], ROUTE_SCALE,
                                           ALU.mult, ALU.mult)
            ins.then_inc(S["sCombTok"], 1)
            for tc in range(16):
                vector.wait_ge(S["sCT"], tc + 1)
                vector.tensor_copy(out=combT.ap()[0:16, 128 * tc:128 * (tc + 1)],
                                   in_=tps[tc % 2][:]).then_inc(S["sCTE"], 1)

            if ph < 2:
                return
            vector.drain()
            def yadd_of(sp):
                ep, tbp = divmod(sp, 4)
                for dc in range(8):
                    gd = sp * 8 + dc
                    vector.wait_ge(S["sYeL"], gd + 1)
                    dst = y_acc.ap()[:, T * dc + 512 * tbp:T * dc + 512 * (tbp + 1)]
                    vector.tensor_add(dst, dst, ye[gd % 2][:]).then_inc(S["sYeF"], 1)

            for s in range(4 * NEX):
                e, tb = divmod(s, 4)
                for pr in range(4):
                    P = s * 4 + pr
                    vector.wait_ge(S["sSl"], P + 1)
                    vector.wait_ge(S["sHpL"], 2 * P + 2)
                    vector.tensor_mul(t1_r[P % 2][:], silu_sb[P % 2][:],
                                      hp_u[P % 2][:]).then_inc(S["sHpF"], 1)
                    if s >= 2 and pr == 0:
                        vector.wait_ge(S["sYeL"], 8 * (s - 1))
                    vector.drain()
                    dst = hpsb[s % 2].ap()[:, 512 * pr:512 * (pr + 1)]
                    if e < NE:
                        if pr == 0:
                            vector.wait_ge(S["sCbM"], s + 1)
                        ins = vector.tensor_mul(dst, t1_r[P % 2][:], cbp[s % 2][:])
                    else:
                        ins = vector.tensor_copy(out=dst, in_=t1_r[P % 2][:])
                    ins.then_inc(S["sHpD"], 1)
                if s >= 1:
                    yadd_of(s - 1)
            yadd_of(4 * NEX - 1)

            if ph < 3:
                return
            for m in range(8 * min(p3n, 2)):
                for nb in range(4):
                    gqk = m * 4 + nb
                    vector.wait_ge(S["sQkM"], gqk + 1)
                    if p3spill and m >= 2 and nb == 0:
                        vector.wait_ge(S[f"sSpl{m % 2}"], 16 * (m // 2))
                    vector.tensor_copy(out=qblk[m % 2].ap()[:, 512 * nb:512 * (nb + 1)],
                                       in_=qk_ps[gqk % 3][:]).then_inc(S["sQkE"], 1)
                vector.drain().then_inc(S["sBlk"], 1)
            vector.memset(
                vhat.ap().rearrange("p (a h d) -> p a h d", h=16, d=65)[:, :, :, 64:65], 1.0)
            for tc in range(16 if p3n >= 3 else 0):
                for nh in range(2):
                    gv = tc * 2 + nh
                    vector.wait_ge(S["sVM"], gv + 1)
                    dst = vhat.ap().rearrange("p (a h d) -> p a h d", h=16, d=65)[
                        :, tc, 8 * nh:8 * (nh + 1), 0:64]
                    src = v_ps[gv % 2].ap().rearrange("p (h d) -> p h d", h=8)
                    vector.tensor_copy(out=dst, in_=src).then_inc(S["sVE"], 1)
            vector.drain().then_inc(S["sP3V"], 1)

            if ph < 4:
                return
            for t1b in range(4):
                for hp in range(8):
                    n = t1b * 8 + hp
                    vector.wait_ge(S["sPTFE"], 8 * (n + 1))
                    with nc.allow_low_precision(reason="bf16 softmax denom recip (bf16-probs class)"):
                        vector.reciprocal(rdenE[:], pvE[64:65, :]).then_inc(S["sRdE"], 1)
                        vector.wait_ge(S["sPTFO"], 8 * (n + 1))
                        vector.reciprocal(rdenO[:], pvO[64:65, :]).then_inc(S["sRdO"], 1)
                    vector.wait_ge(S["sBcAE"], n + 1)
                    vector.tensor_copy(out=bcE_sb[:], in_=bcyo[0][0:64, :])
                    vector.drain()
                    vector.tensor_mul(attnN.ap()[0:64, 512 * hp:512 * (hp + 1)],
                                      pvE[0:64, :], bcE_sb[:]).then_inc(S["sPvFE"], 1)
                    vector.wait_ge(S["sBcAO"], n + 1)
                    vector.tensor_copy(out=bcO_sb[:], in_=bcyo[1][0:64, :])
                    vector.drain()
                    vector.tensor_mul(attnN.ap()[64:128, 512 * hp:512 * (hp + 1)],
                                      pvO[0:64, :], bcO_sb[:]).then_inc(S["sPvFO"], 1)
                for dc in range(8):
                    gd = t1b * 8 + dc
                    vector.wait_ge(S["sYoL"], gd + 1)
                    dst = y_acc.ap()[:, T * dc + 512 * t1b:T * dc + 512 * (t1b + 1)]
                    vector.tensor_add(dst, dst, bcyo[gd % 2][:]).then_inc(S["sYoF"], 1)

    return nc


_NC = None
_PH = 4
_P3N = 3
_P3SPILL = True


def kernel(L0_attn_x, L0_attn_w_qkv, L0_attn_w_out, L0_attn_b_out,
           L0_ffn_input_ids, L0_ffn_gate_w, L0_ffn_gate_b, L0_ffn_w13,
           L0_ffn_w2, L0_ffn_w13_s, L0_ffn_w2_s):
    global _NC
    if _NC is None:
        _NC = build_nc(_PH, _P3N, _P3SPILL)
    nc = _NC

    x = np.asarray(L0_attn_x, dtype=np.float32)
    wqkv = np.ascontiguousarray(np.asarray(L0_attn_w_qkv, dtype=np.float32))
    wout = np.ascontiguousarray(np.asarray(L0_attn_w_out, dtype=np.float32))
    bout = np.ascontiguousarray(
        np.asarray(L0_attn_b_out, dtype=np.float32).reshape(8, 128).T)
    gwT = np.ascontiguousarray(np.asarray(L0_ffn_gate_w, dtype=np.float32).T)
    gb = np.ascontiguousarray(np.tile(np.asarray(L0_ffn_gate_b, dtype=np.float32).reshape(1, 16), (1, 16)))
    w13T = np.ascontiguousarray(np.asarray(L0_ffn_w13, dtype=np.float32).transpose(0, 2, 1))
    w2T = np.ascontiguousarray(np.asarray(L0_ffn_w2, dtype=np.float32).transpose(0, 2, 1))
    w13sT = np.ascontiguousarray(np.asarray(L0_ffn_w13_s, dtype=np.float32).T)
    w2sT = np.ascontiguousarray(np.asarray(L0_ffn_w2_s, dtype=np.float32).T)
    revio = np.ascontiguousarray(
        np.broadcast_to(np.tile(np.arange(16, 0, -1, dtype=np.float32), 16), (128, 256)))
    ones1f = np.ones((1, 128), np.float32)
    ones1b = np.ones((1, 128), ml_dtypes.bfloat16)
    onescol = np.ones((128, 1), ml_dtypes.bfloat16)
    ident = np.eye(128, dtype=np.float32)
    ehot = np.zeros((16, 16 * 128), ml_dtypes.bfloat16)
    for e in range(16):
        ehot[e, 128 * e:128 * (e + 1)] = 1


    in_maps = []
    for i in range(8):
        in_maps.append(dict(
            xT=np.ascontiguousarray(x[i].T), wqkv=wqkv, wout=wout, bout=bout,
            gwT=gwT, gb=gb, w13T=w13T, w2T=w2T, w13sT=w13sT, w2sT=w2sT,
            revio=revio, ones1f=ones1f, ones1b=ones1b, onescol=onescol,
            ident=ident, ehot=ehot))
    res = run_bass_kernel_spmd(nc, in_maps, core_ids=list(range(8)))
    out = np.stack([np.asarray(res.results[i]["outT"], dtype=np.float32).T
                    for i in range(8)])
    return np.ascontiguousarray(out)
